# version 38 (final)
# baseline (speedup 1.0000x reference)
import numpy as np
from contextlib import ExitStack
from collections import defaultdict

import concourse.bass as bass
import concourse.mybir as mybir
from concourse.bass_utils import run_bass_kernel_spmd

F32 = mybir.dt.float32
BF16 = mybir.dt.bfloat16
U8 = mybir.dt.uint8
AF = mybir.ActivationFunctionType
ALU = mybir.AluOpType
AX = mybir.AxisListType

ENGINES = ("sync", "gpsimd", "scalar", "vector", "tensor")


class Op:
    __slots__ = ("engine", "fn", "deps", "strict", "is_dma", "seq", "inc",
                 "semval_after", "waits")

    def __init__(self, engine, fn, deps, is_dma, seq):
        self.engine = engine
        self.fn = fn
        self.deps = deps
        self.strict = set()
        self.is_dma = is_dma
        self.seq = seq
        self.inc = 0
        self.semval_after = 0
        self.waits = []


class Buf:
    __slots__ = ("writers", "readers", "name")

    def __init__(self, name="buf", init_deps=()):
        self.name = name
        self.writers = list(init_deps)
        self.readers = []


class Prog:
    def __init__(self, nc):
        self.nc = nc
        self.ops = []

    def op(self, engine, fn, reads=(), writes=(), deps=(), scalar_reads=()):
        return self._add(engine, fn, reads, writes, deps, scalar_reads, False)

    def dma(self, engine, fn, reads=(), writes=(), deps=()):
        return self._add(engine, fn, reads, writes, deps, (), True)

    def _add(self, engine, fn, reads, writes, deps, scalar_reads, is_dma):
        d = list(deps)
        strict = set()
        for b in reads:
            d.extend(b.writers)
        for b in scalar_reads:
            d.extend(b.writers)
            strict.update(w.seq for w in b.writers)
        wdeps = []
        for b in writes:
            wdeps.extend(b.readers)
            wdeps.extend(b.writers)
        o = Op(engine, fn, d + wdeps, is_dma, len(self.ops))
        o.strict = strict
        self.ops.append(o)
        for b in reads:
            b.readers.append(o)
        for b in scalar_reads:
            b.readers.append(o)
        for b in writes:
            b.writers = [o]
            b.readers = []
        return o

    def finalize(self):
        for o in self.ops:
            if o.is_dma:
                o.inc = 16
            for d in o.deps:
                if d.inc:
                    continue
                if d.is_dma or d.engine != o.engine or d.seq in o.strict:
                    d.inc = 16 if d.is_dma else 1
        semval = defaultdict(int)
        for o in self.ops:
            semval[o.engine] += o.inc
            o.semval_after = semval[o.engine]
        for o in self.ops:
            need = {}
            for d in o.deps:
                if (d.engine == o.engine and not d.is_dma
                        and d.seq not in o.strict):
                    continue
                if d.engine == o.engine and d.is_dma and o.is_dma:
                    continue
                if d.semval_after > need.get(d.engine, 0):
                    need[d.engine] = d.semval_after
            o.waits = sorted(need.items())
        maxseen = defaultdict(lambda: defaultdict(int))
        nwaits = 0
        for o in self.ops:
            seen = maxseen[o.engine]
            w = []
            for e, v in o.waits:
                if v > seen[e]:
                    w.append((e, v))
                    seen[e] = v
            o.waits = w
            nwaits += len(w)
        return dict(semval), nwaits

    def emit(self, sems):
        byeng = defaultdict(list)
        for o in self.ops:
            byeng[o.engine].append(o)

        def run(eng_obj, engine_name):
            for o in byeng[engine_name]:
                for e, v in o.waits:
                    eng_obj.wait_ge(sems[e], v)
                ins = o.fn(eng_obj)
                if o.inc:
                    ins.then_inc(sems[o.engine], o.inc)
        return run


B, S, DM = 4, 2048, 2048
H, HD, NH = 16, 128, 8
KC = 16
NQT = 16
NIT = 64
GRP = 8
NG = NIT // GRP
EXP_SHIFT = -12.0
QK_SCALE = 1.0 / float(np.sqrt(128.0))
RMS_EPS = 1.1920928955078125e-07
LN_EPS = 1e-5
LAM_INIT = 0.8
RMS_GAIN_BIAS = float(np.log(0.2 * np.sqrt(128.0)))

REG_A = 16512
REG_B = REG_A + 65536
REG_C = REG_B + 32768
REG_D = REG_C + 32768
REG_E = REG_D + 32768
PERSIST = 226304
SBUF_TOP = 229344


def build_nc():
    nc = bass.Bass("TRN2", target_bir_lowering=False)

    x_d = nc.dram_tensor("x", [S, DM], F32, kind="ExternalInput")
    wq_d = nc.dram_tensor("wq", [NH, 128, KC, 128], F32, kind="ExternalInput")
    wk_d = nc.dram_tensor("wk", [NH, 128, KC, 128], F32, kind="ExternalInput")
    wv2_d = nc.dram_tensor("wv2", [NH // 2, 128, KC, 256], F32, kind="ExternalInput")
    wo_d = nc.dram_tensor("wo", [2, KC, 128, 1024], F32, kind="ExternalInput")
    gw_d = nc.dram_tensor("gw", [NIT, 128, KC, 128], U8, kind="ExternalInput")
    uw_d = nc.dram_tensor("uw", [NIT, 128, KC, 128], U8, kind="ExternalInput")
    dw_d = nc.dram_tensor("dw", [NIT, 128, DM], U8, kind="ExternalInput")
    gs_d = nc.dram_tensor("gs", [128, NIT], F32, kind="ExternalInput")
    us_d = nc.dram_tensor("us", [128, NIT], F32, kind="ExternalInput")
    dsb_d = nc.dram_tensor("dsb", [128, DM], F32, kind="ExternalInput")
    lam_d = nc.dram_tensor("lam4", [128, 4], F32, kind="ExternalInput")
    satscr = nc.dram_tensor("satscr", [NH * 128, DM], BF16, kind="Internal")
    out_d = nc.dram_tensor("out", [NH * 128, DM], F32, kind="ExternalOutput")

    at = nc.alloc_sbuf_tensor_at
    xn_fm = at("xn_fm", [128, KC * 2048], BF16, offset=REG_A)
    wo_h = [at("wo_h0", [128, KC * 1024], BF16, offset=REG_A),
            at("wo_h1", [128, KC * 1024], BF16, offset=REG_A + 32768)]
    out_acc = at("out_acc", [128, NH * 2048], F32, offset=REG_A)
    xn_ffn = at("xn_ffn", [128, KC * 1024], BF16, offset=REG_B)
    normed = at("normed", [128, NH * NQT * 128], BF16, offset=REG_C)
    hact = at("hact", [128, GRP * 1024], BF16, offset=REG_C)
    gbf = [at("gbf0", [128, 2048], BF16, offset=REG_C + 16384),
           at("gbf1", [128, 2048], BF16, offset=REG_C + 20480)]
    ubf = [at("ubf0", [128, 2048], BF16, offset=REG_C + 24576),
           at("ubf1", [128, 2048], BF16, offset=REG_C + 28672)]
    sat_st = [at("sat_st0", [128, 1024], BF16, offset=REG_D),
              at("sat_st1", [128, 1024], BF16, offset=REG_D + 2048)]
    dwbf = at("dwbf", [128, GRP * 2048], BF16, offset=REG_D)
    xt = [at("xt0", [128, 2048], F32, offset=REG_E),
          at("xt1", [128, 2048], F32, offset=REG_E + 8192)]
    xnt = [at("xnt0", [128, 2048], BF16, offset=REG_E + 16384),
           at("xnt1", [128, 2048], BF16, offset=REG_E + 20480)]
    junk2k = at("junk2k", [128, 2048], BF16, offset=REG_E + 24576)
    wsA = at("wsA", [128, 2048], BF16, offset=REG_E)
    wsB = at("wsB", [128, 2048], BF16, offset=REG_E + 4096)
    v2 = at("v2", [128, 4096], BF16, offset=REG_E + 8192)
    wv2s = at("wv2s", [128, 4096], BF16, offset=REG_E + 16384)
    qfm = at("qfm", [128, 2048], BF16, offset=REG_E + 16384)
    kfm = at("kfm", [128, 2048], BF16, offset=REG_E + 20480)
    e1r = [at("e1_0", [128, 2048], BF16, offset=REG_E + 24576),
           at("e1_1", [128, 2048], BF16, offset=REG_E + 40960)]
    e2r = [at("e2_0", [128, 2048], BF16, offset=REG_E + 28672),
           at("e2_1", [128, 2048], BF16, offset=REG_E + 32768),
           at("e2_2", [128, 2048], BF16, offset=REG_E + 36864)]
    aTa = at("aTa", [128, 1024], BF16, offset=REG_D + 4096)
    aTb = at("aTb", [128, 1024], BF16, offset=REG_D + 6144)
    gw8 = [at("gw80", [128, 2048], U8, offset=REG_E),
           at("gw81", [128, 2048], U8, offset=REG_E + 2048)]
    uw8 = [at("uw80", [128, 2048], U8, offset=REG_E + 4096),
           at("uw81", [128, 2048], U8, offset=REG_E + 6144)]
    dw8 = [at("dw80", [128, 2048], U8, offset=REG_E + 8192),
           at("dw81", [128, 2048], U8, offset=REG_E + 10240)]
    gsil = [at("gsil0", [128, 1024], BF16, offset=REG_E + 12288),
            at("gsil1", [128, 1024], BF16, offset=REG_E + 14336)]
    gs_sb = at("gs_sb", [128, NIT], F32, offset=REG_E + 16384)
    us_sb = at("us_sb", [128, NIT], F32, offset=REG_E + 16640)
    xr = at("xr", [128, 2048], F32, offset=REG_E)
    ta = at("ta", [128, 2048], F32, offset=REG_E + 8192)
    tb = at("tb", [128, 2048], F32, offset=REG_E + 16384)
    sat = [at("sat0", [128, 2048], BF16, offset=REG_E + 24576),
           at("sat1", [128, 2048], BF16, offset=REG_E + 28672)]
    dsb_sb = at("dsb_sb", [128, 2048], F32, offset=REG_E + 32768)
    stats = at("stats", [128, 64], F32, offset=PERSIST)
    ones = at("ones", [1, 128], F32, offset=PERSIST + 256)
    lam_sb = at("lam_sb", [128, 4], F32, offset=PERSIST + 768)
    id_sb = at("id_sb", [128, 128], BF16, offset=PERSIST + 800)
    junk1 = at("junk1", [128, 128], BF16, offset=PERSIST + 1056)
    cellc = at("cellc", [128, 4], F32, offset=PERSIST + 1312)
    assert PERSIST + 1344 <= SBUF_TOP

    psA = nc.alloc_psum_tensor("psA", [128, 2048], F32)
    psB = nc.alloc_psum_tensor("psB", [128, 2048], F32)
    psA_bfv = psA[:, 0:1024].bitcast(BF16)
    vb4 = psB[:, 0:512].bitcast(BF16)
    vb5 = psB[:, 512:1024].bitcast(BF16)
    AVC = 1024

    p = Prog(nc)
    C_SUM, C_SQ, C_MEAN, C_M2, C_MSQ, C_VAR = 0, 1, 2, 3, 4, 5
    C_LNV, C_RSTD = 6, 7
    Q_S1A, Q_S1B, Q_S2A, Q_S2B, Q_S1, Q_S2, Q_R2, Q_T, Q_NEGC = range(9)
    Q_T1, Q_DD, Q_LND, Q_SCQ = 9, 10, 11, 12
    C_ED1, C_ED2, C_LAM, C_NLAM1, C_NLAM = 56, 57, 58, 59, 60

    def scol(c):
        return stats[:, c:c + 1]

    regions = defaultdict(list)

    def newbuf(name, region=None):
        b = Buf(name)
        if region is not None:
            regions[region].append(b)
        return b

    def roll(region):
        deps = []
        for b in regions[region]:
            deps.extend(b.writers)
            deps.extend(b.readers)
        regions[region] = []

        def make(name):
            b = Buf(name, init_deps=list(deps))
            regions[region].append(b)
            return b
        return make

    b_x = [newbuf("xt0", "E"), newbuf("xt1", "E")]
    b_xnt = [newbuf("xnt0", "E"), newbuf("xnt1", "E")]
    b_junk2k = newbuf("junk2k", "E")
    b_junk1 = newbuf("junk1")
    b_stats = {c: newbuf(f"st{c}") for c in range(64)}
    b_ones = newbuf("ones")
    b_lam = newbuf("lam")
    b_id = newbuf("id")
    b_xn = newbuf("xn_fm", "A")
    b_xnffn = newbuf("xn_ffn")
    b_normed = newbuf("normed", "C")
    b_psA_lo = newbuf("psA_lo")
    b_psA_hi = newbuf("psA_hi")
    b_psBb = [newbuf(f"psB{i}") for i in range(4)]
    b_psB_lo = [b_psBb[0], b_psBb[1]]
    b_psB_hi = [b_psBb[2], b_psBb[3]]
    PSQ = [[b_psA_lo], [b_psA_hi], b_psB_lo, b_psB_hi]

    p.dma("sync", lambda e: e.dma_start(out=lam_sb[:, :], in_=lam_d[:, :]),
          writes=[b_lam])
    b_cellc = newbuf("cellc")
    p.op("gpsimd", lambda e: e.memset(cellc[:, 0:1], LN_EPS), writes=[b_cellc])
    p.op("gpsimd", lambda e: e.memset(cellc[:, 1:2], EXP_SHIFT),
         reads=[b_cellc], writes=[b_cellc])
    p.op("gpsimd", lambda e: e.memset(cellc[:, 2:3], RMS_GAIN_BIAS),
         reads=[b_cellc], writes=[b_cellc])
    p.op("gpsimd", lambda e: e.memset(ones[:, :], 1.0), writes=[b_ones])
    p.op("gpsimd", lambda e: e.memset(id_sb[:, :], 0.0), writes=[b_id])
    p.op("gpsimd", lambda e: e.affine_select(
        out=id_sb[:, :], in_=id_sb[:, :], compare_op=ALU.not_equal,
        fill=1.0, base=0, pattern=[[-1, 128]], channel_multiplier=1),
        reads=[b_id], writes=[b_id])
    p.op("tensor", lambda e: e.matmul(psA[0:1, 0:1], lam_sb[:, 0:1], lam_sb[:, 2:3],
                                      start=True, stop=True),
         reads=[b_lam], writes=[b_psA_lo])
    p.op("tensor", lambda e: e.matmul(psA[0:1, 1:2], lam_sb[:, 1:2], lam_sb[:, 3:4],
                                      start=True, stop=True),
         reads=[b_lam], writes=[b_psA_lo])
    p.op("scalar", lambda e: e.activation(out=stats[0:1, C_ED1:C_ED1 + 1],
                                          in_=psA[0:1, 0:1], func=AF.Exp),
         reads=[b_psA_lo], writes=[b_stats[C_ED1]])
    p.op("scalar", lambda e: e.activation(out=stats[0:1, C_ED2:C_ED2 + 1],
                                          in_=psA[0:1, 1:2], func=AF.Exp),
         reads=[b_psA_lo], writes=[b_stats[C_ED2]])
    p.op("vector", lambda e: e.tensor_sub(stats[0:1, C_LAM:C_LAM + 1],
                                          stats[0:1, C_ED1:C_ED1 + 1],
                                          stats[0:1, C_ED2:C_ED2 + 1]),
         scalar_reads=[b_stats[C_ED1], b_stats[C_ED2]], writes=[b_stats[C_LAM]])
    p.op("vector", lambda e: e.tensor_scalar(
        out=stats[0:1, C_NLAM1:C_NLAM1 + 1], in0=stats[0:1, C_LAM:C_LAM + 1],
        scalar1=LAM_INIT, scalar2=-1.0, op0=ALU.add, op1=ALU.mult),
        scalar_reads=[b_stats[C_LAM]], writes=[b_stats[C_NLAM1]])
    p.op("tensor", lambda e: e.matmul(psB[:, 0:1], ones[:, :],
                                      stats[0:1, C_NLAM1:C_NLAM1 + 1],
                                      start=True, stop=True),
         reads=[b_ones], scalar_reads=[b_stats[C_NLAM1]], writes=[b_psBb[0]])
    p.op("vector", lambda e: e.tensor_copy(out=scol(C_NLAM), in_=psB[:, 0:1]),
         reads=[b_psBb[0]], writes=[b_stats[C_NLAM]])

    for t in range(KC):
        s = t % 2
        bx, bn = b_x[s], b_xnt[s]
        p.dma("sync", lambda e, t=t, s=s: e.dma_start(
            out=xt[s][:, :], in_=x_d[t * 128:(t + 1) * 128, :]), writes=[bx])
        p.op("vector", lambda e, s=s: e.reduce_sum(scol(C_SUM), xt[s][:, :], axis=AX.X),
             reads=[bx], writes=[b_stats[C_SUM]])
        p.op("scalar", lambda e, s=s: e.activation(
            out=junk2k[:, :], in_=xt[s][:, :], func=AF.Square,
            accum_out=scol(C_SQ)),
            reads=[bx], writes=[b_junk2k, b_stats[C_SQ]])
        p.op("vector", lambda e: e.tensor_scalar_mul(scol(C_MEAN), scol(C_SUM),
                                                     1.0 / DM),
             scalar_reads=[b_stats[C_SUM]], writes=[b_stats[C_MEAN]])
        p.op("vector", lambda e: e.tensor_scalar_mul(scol(C_M2), scol(C_SQ),
                                                     1.0 / DM),
             scalar_reads=[b_stats[C_SQ]], writes=[b_stats[C_M2]])
        p.op("vector", lambda e: e.tensor_mul(scol(C_MSQ), scol(C_MEAN), scol(C_MEAN)),
             scalar_reads=[b_stats[C_MEAN]], writes=[b_stats[C_MSQ]])
        p.op("vector", lambda e: e.tensor_sub(scol(C_VAR), scol(C_M2), scol(C_MSQ)),
             scalar_reads=[b_stats[C_M2], b_stats[C_MSQ]], writes=[b_stats[C_VAR]])
        p.op("scalar", lambda e: e.activation(out=scol(C_LNV), in_=scol(C_VAR),
                                              func=AF.Ln, bias=cellc[:, 0:1]),
             scalar_reads=[b_stats[C_VAR], b_cellc],
             writes=[b_stats[C_LNV]])
        p.op("scalar", lambda e: e.activation(out=scol(C_RSTD), in_=scol(C_LNV),
                                              func=AF.Exp, scale=-0.5),
             reads=[b_stats[C_LNV]], writes=[b_stats[C_RSTD]])
        p.op("vector", lambda e, s=s: e.tensor_scalar(
            out=xnt[s][:, :], in0=xt[s][:, :], scalar1=scol(C_MEAN),
            scalar2=scol(C_RSTD), op0=ALU.subtract, op1=ALU.mult),
            reads=[bx], scalar_reads=[b_stats[C_MEAN], b_stats[C_RSTD]],
            writes=[bn])
        for c in range(KC):
            p.op("tensor", lambda e, s=s, c=c: e.transpose(
                psA_bfv[:, c * 128:(c + 1) * 128],
                xnt[s][:, c * 128:(c + 1) * 128], id_sb[:, :]),
                reads=[bn, b_id], writes=[b_psA_lo])
        p.op("vector", lambda e, t=t: e.tensor_copy(
            out=xn_fm[:, :].rearrange("p (c n) -> p c n", n=2048)
            [:, :, t * 128:(t + 1) * 128],
            in_=psA_bfv[:, :]),
            reads=[b_psA_lo], writes=[b_xn])
    for c in range(KC):
        p.op("vector", lambda e, c=c: e.tensor_copy(
            out=xn_ffn[:, c * 1024:(c + 1) * 1024],
            in_=xn_fm[:, c * 2048:c * 2048 + 1024]),
            reads=[b_xn], writes=[b_xnffn])

    mkE = roll("E")
    b_wsA = mkE("wsA")
    b_wsB = mkE("wsB")
    b_wv2 = mkE("wv2")
    b_v2 = mkE("v2")
    b_qfm = mkE("qfm")
    b_kfm = mkE("kfm")
    b_e1r = [mkE("e1_0"), mkE("e1_1")]
    b_e2r = [[mkE(f"e2_{r}_{kh}") for kh in range(2)] for r in range(3)]
    b_aTa = newbuf("aTa", "D")
    b_aTb = newbuf("aTb", "D")

    for h in range(NH):
        par = h % 2
        p.dma("gpsimd", lambda e, h=h: e.dma_start(out=wsA[:, :],
                                                   in_=wq_d[h, :, :, :]),
              writes=[b_wsA])
        p.dma("gpsimd", lambda e, h=h: e.dma_start(out=wsB[:, :],
                                                   in_=wk_d[h, :, :, :]),
              writes=[b_wsB])
        if par == 0:
            p.dma("gpsimd", lambda e, h=h: e.dma_start(
                out=wv2s[:, :], in_=wv2_d[h // 2, :, :, :]),
                writes=[b_wv2, b_qfm, b_kfm])
            for half in (0, 1):
                pst = psA if half == 0 else psB
                for tt in range(8):
                    ttg = half * 8 + tt
                    breg = ((b_psA_lo if tt < 4 else b_psA_hi) if half == 0
                            else b_psBb[tt // 2])
                    for kc in range(KC):
                        p.op("tensor", lambda e, pst=pst, tt=tt, ttg=ttg, kc=kc:
                             e.matmul(pst[:, tt * 256:(tt + 1) * 256],
                                      xn_fm[:, kc * 2048 + ttg * 128:
                                            kc * 2048 + (ttg + 1) * 128],
                                      wv2s[:, kc * 256:(kc + 1) * 256],
                                      start=(kc == 0), stop=(kc == KC - 1)),
                             reads=[b_xn, b_wv2], writes=[breg])
            p.op("vector", lambda e: e.tensor_copy(out=v2[:, 0:2048], in_=psA[:, :]),
                 reads=[b_psA_lo, b_psA_hi], writes=[b_v2])
            p.op("vector", lambda e: e.tensor_copy(out=v2[:, 2048:4096], in_=psB[:, :]),
                 reads=b_psBb, writes=[b_v2])
        for n in range(4):
            breg = b_psA_lo if n < 2 else b_psA_hi
            for kc in range(KC):
                p.op("tensor", lambda e, n=n, kc=kc: e.matmul(
                    psA[:, n * 512:(n + 1) * 512],
                    wsA[:, kc * 128:(kc + 1) * 128],
                    xn_fm[:, kc * 2048 + n * 512:kc * 2048 + (n + 1) * 512],
                    start=(kc == 0), stop=(kc == KC - 1)),
                    reads=[b_wsA, b_xn], writes=[breg])
        p.op("vector", lambda e: e.tensor_scalar_mul(qfm[:, :], psA[:, :], QK_SCALE),
             reads=[b_psA_lo, b_psA_hi], writes=[b_qfm, b_wv2])
        for n in range(4):
            breg = b_psBb[n]
            for kc in range(KC):
                p.op("tensor", lambda e, n=n, kc=kc: e.matmul(
                    psB[:, n * 512:(n + 1) * 512],
                    wsB[:, kc * 128:(kc + 1) * 128],
                    xn_fm[:, kc * 2048 + n * 512:kc * 2048 + (n + 1) * 512],
                    start=(kc == 0), stop=(kc == KC - 1)),
                    reads=[b_wsB, b_xn], writes=[breg])
        p.op("vector", lambda e: e.tensor_copy(out=kfm[:, :], in_=psB[:, :]),
             reads=b_psBb, writes=[b_kfm, b_wv2])

        def emit_scores_half(qt, kh):
            rot = qt % 3
            cb = 8 + 13 * rot
            for n in range(2):
                p.op("tensor", lambda e, qt=qt, kh=kh, n=n: e.matmul(
                    psA[:, n * 512:(n + 1) * 512],
                    qfm[0:64, qt * 128:(qt + 1) * 128],
                    kfm[0:64, kh * 1024 + n * 512:kh * 1024 + (n + 1) * 512],
                    start=True, stop=True),
                    reads=[b_qfm, b_kfm], writes=[b_psA_lo])
            for n in range(2):
                p.op("tensor", lambda e, qt=qt, kh=kh, n=n: e.matmul(
                    psA[:, 1024 + n * 512:1024 + (n + 1) * 512],
                    qfm[64:128, qt * 128:(qt + 1) * 128],
                    kfm[64:128, kh * 1024 + n * 512:kh * 1024 + (n + 1) * 512],
                    start=True, stop=True),
                    reads=[b_qfm, b_kfm], writes=[b_psA_hi])
            c1 = cb + (Q_S1A if kh == 0 else Q_S1B)
            c2 = cb + (Q_S2A if kh == 0 else Q_S2B)
            pe1 = qt % 2
            p.op("scalar", lambda e, kh=kh, c1=c1, pe1=pe1: e.activation(
                out=e1r[pe1][:, kh * 1024:(kh + 1) * 1024], in_=psA[:, 0:1024],
                func=AF.Exp, bias=cellc[:, 1:2], accum_out=scol(c1)),
                reads=[b_psA_lo], scalar_reads=[b_cellc],
                writes=[b_e1r[pe1], b_stats[c1]])
            p.op("scalar", lambda e, kh=kh, rot=rot, c2=c2: e.activation(
                out=e2r[rot][:, kh * 1024:(kh + 1) * 1024], in_=psA[:, 1024:2048],
                func=AF.Exp, bias=cellc[:, 1:2], accum_out=scol(c2)),
                reads=[b_psA_hi], scalar_reads=[b_cellc],
                writes=[b_e2r[rot][kh], b_stats[c2]])

        def emit_combine(qt):
            rot = qt % 3
            cb = 8 + 13 * rot
            p.op("vector", lambda e, cb=cb: e.tensor_add(
                scol(cb + Q_S1), scol(cb + Q_S1A), scol(cb + Q_S1B)),
                scalar_reads=[b_stats[cb + Q_S1A], b_stats[cb + Q_S1B]],
                writes=[b_stats[cb + Q_S1]])
            p.op("vector", lambda e, cb=cb: e.tensor_add(
                scol(cb + Q_S2), scol(cb + Q_S2A), scol(cb + Q_S2B)),
                scalar_reads=[b_stats[cb + Q_S2A], b_stats[cb + Q_S2B]],
                writes=[b_stats[cb + Q_S2]])
            p.op("vector", lambda e, cb=cb: e.reciprocal(scol(cb + Q_R2),
                                                         scol(cb + Q_S2)),
                 scalar_reads=[b_stats[cb + Q_S2]], writes=[b_stats[cb + Q_R2]])
            p.op("vector", lambda e, cb=cb: e.tensor_mul(
                scol(cb + Q_T), scol(cb + Q_S1), scol(cb + Q_R2)),
                scalar_reads=[b_stats[cb + Q_S1], b_stats[cb + Q_R2]],
                writes=[b_stats[cb + Q_T]])
            p.op("vector", lambda e, cb=cb: e.tensor_mul(
                scol(cb + Q_NEGC), scol(cb + Q_T), scol(C_NLAM)),
                scalar_reads=[b_stats[cb + Q_T], b_stats[C_NLAM]],
                writes=[b_stats[cb + Q_NEGC]])
            pe1 = qt % 2
            for kh in range(2):
                p.op("vector", lambda e, rot=rot, kh=kh, cb=cb:
                     e.tensor_scalar_mul(
                        e2r[rot][:, kh * 1024:(kh + 1) * 1024],
                        e2r[rot][:, kh * 1024:(kh + 1) * 1024],
                        scol(cb + Q_NEGC)),
                     reads=[b_e2r[rot][kh]],
                     scalar_reads=[b_stats[cb + Q_NEGC]],
                     writes=[b_e2r[rot][kh]])
                p.op("vector", lambda e, rot=rot, kh=kh, pe1=pe1:
                     e.tensor_add(
                        e2r[rot][:, kh * 1024:(kh + 1) * 1024],
                        e2r[rot][:, kh * 1024:(kh + 1) * 1024],
                        e1r[pe1][:, kh * 1024:(kh + 1) * 1024]),
                     reads=[b_e2r[rot][kh], b_e1r[pe1]],
                     writes=[b_e2r[rot][kh]])

        def emit_tail_a(qt, h, par):
            rot = qt % 3
            a = e2r[rot]
            ba = b_e2r[rot][0]
            for kt in range(8):
                p.op("tensor", lambda e, kt=kt, a=a, ba=ba: e.transpose(
                    vb4[:, kt * 128:(kt + 1) * 128],
                    a[:, kt * 128:(kt + 1) * 128], id_sb[:, :]),
                    reads=[ba, b_id], writes=[b_psBb[0]])
            p.op("vector", lambda e: e.tensor_copy(
                out=aTa[:, :], in_=vb4[:, :]),
                reads=[b_psBb[0]], writes=[b_aTa])

        def emit_tail_av_a(qt, h, par):
            avc = AVC + 512 * (qt % 2)
            bav = b_psBb[2 + qt % 2]
            for kt in range(8):
                p.op("tensor", lambda e, kt=kt, par=par, avc=avc: e.matmul(
                    psB[:, avc:avc + 128],
                    aTa[:, kt * 128:(kt + 1) * 128],
                    v2[:, kt * 256 + par * 128:kt * 256 + par * 128 + 128],
                    start=(kt == 0), stop=False),
                    reads=[b_aTa, b_v2], writes=[bav])

        def emit_tail_b(qt, h, par):
            rot = qt % 3
            cb = 8 + 13 * rot
            avc = AVC + 512 * (qt % 2)
            bav = b_psBb[2 + qt % 2]
            a = e2r[rot]
            ba = b_e2r[rot][1]
            for kt in range(8):
                ktg = 8 + kt
                p.op("tensor", lambda e, kt=kt, ktg=ktg, a=a, ba=ba: e.transpose(
                    vb5[:, kt * 128:(kt + 1) * 128],
                    a[:, ktg * 128:(ktg + 1) * 128], id_sb[:, :]),
                    reads=[ba, b_id], writes=[b_psBb[1]])
            p.op("vector", lambda e: e.tensor_copy(
                out=aTb[:, :], in_=vb5[:, :]),
                reads=[b_psBb[1]], writes=[b_aTb])
            for kt in range(8):
                ktg = 8 + kt
                p.op("tensor", lambda e, kt=kt, ktg=ktg, par=par, avc=avc: e.matmul(
                    psB[:, avc:avc + 128],
                    aTb[:, kt * 128:(kt + 1) * 128],
                    v2[:, ktg * 256 + par * 128:ktg * 256 + par * 128 + 128],
                    start=False, stop=(ktg == 15)),
                    reads=[b_aTb, b_v2], writes=[bav])
            p.op("scalar", lambda e, cb=cb, avc=avc: e.activation(
                out=junk1[:, :], in_=psB[:, avc:avc + 128], func=AF.Square,
                accum_out=scol(cb + Q_T1)),
                reads=[bav], writes=[b_junk1, b_stats[cb + Q_T1]])
            p.op("vector", lambda e, cb=cb: e.tensor_mul(
                scol(cb + Q_DD), scol(cb + Q_S1), scol(cb + Q_S1)),
                scalar_reads=[b_stats[cb + Q_S1]], writes=[b_stats[cb + Q_DD]])
            p.op("vector", lambda e, cb=cb: e.scalar_tensor_tensor(
                out=scol(cb + Q_DD), in0=scol(cb + Q_DD), scalar=128.0 * RMS_EPS,
                in1=scol(cb + Q_T1), op0=ALU.mult, op1=ALU.add),
                scalar_reads=[b_stats[cb + Q_DD], b_stats[cb + Q_T1]],
                writes=[b_stats[cb + Q_DD]])
            p.op("scalar", lambda e, cb=cb: e.activation(
                out=scol(cb + Q_LND), in_=scol(cb + Q_DD), func=AF.Ln),
                scalar_reads=[b_stats[cb + Q_DD]], writes=[b_stats[cb + Q_LND]])
            p.op("scalar", lambda e, cb=cb: e.activation(
                out=scol(cb + Q_SCQ), in_=scol(cb + Q_LND), func=AF.Exp,
                scale=-0.5, bias=cellc[:, 2:3]),
                reads=[b_stats[cb + Q_LND]], scalar_reads=[b_cellc],
                writes=[b_stats[cb + Q_SCQ]])
            p.op("vector", lambda e, h=h, qt=qt, cb=cb, avc=avc: e.tensor_scalar_mul(
                normed[:, (h * NQT + qt) * 128:(h * NQT + qt) * 128 + 128],
                psB[:, avc:avc + 128], scol(cb + Q_SCQ)),
                reads=[bav], scalar_reads=[b_stats[cb + Q_SCQ]],
                writes=[b_normed])

        def fill(n, qt):
            avc = AVC + 512 * ((qt + 1) % 2)
            bav = b_psBb[2 + (qt + 1) % 2]
            for _ in range(n):
                p.op("tensor", lambda e, avc=avc: e.matmul(
                    psB[:, avc:avc + 128], id_sb[:, :], id_sb[:, :],
                    start=True, stop=True),
                    reads=[b_id], writes=[bav])

        LAG = 2
        for qt in range(NQT + LAG):
            if qt < NQT:
                emit_scores_half(qt, 0)
            fill(10, qt)
            if qt >= LAG:
                emit_tail_a(qt - LAG, h, par)
            if qt < NQT:
                emit_scores_half(qt, 1)
            fill(10, qt)
            if qt >= LAG:
                emit_tail_av_a(qt - LAG, h, par)
                emit_tail_b(qt - LAG, h, par)
            if qt < NQT:
                emit_combine(qt)
            fill(6, qt)

    mkA = roll("A")
    b_woh = [mkA("wo_h0"), mkA("wo_h1")]
    b_sat_st = [newbuf("sat_st0", "D"), newbuf("sat_st1", "D")]
    b_satscr = newbuf("satscr")
    for half in range(2):
        for tc in range(KC):
            p.dma("gpsimd", lambda e, half=half, tc=tc: e.dma_start(
                out=wo_h[half][:, tc * 1024:(tc + 1) * 1024],
                in_=wo_d[half, tc, :, :]),
                writes=[b_woh[half]])
    for half in range(2):
        for h in range(NH):
            quad = (half * NH + h) % 4
            pst = psA if quad < 2 else psB
            lo = (quad % 2) * 1024
            breg = PSQ[quad]
            for n2 in range(2):
                for tc in range(KC):
                    p.op("tensor", lambda e, pst=pst, lo=lo, n2=n2, tc=tc, h=h, half=half:
                         e.matmul(pst[:, lo + n2 * 512:lo + (n2 + 1) * 512],
                                  normed[:, (h * NQT + tc) * 128:(h * NQT + tc) * 128 + 128],
                                  wo_h[half][:, tc * 1024 + n2 * 512:tc * 1024 + (n2 + 1) * 512],
                                  start=(tc == 0), stop=(tc == KC - 1)),
                         reads=[b_normed, b_woh[half]], writes=breg)
            sp = (half * NH + h) % 2
            p.op("vector", lambda e, pst=pst, lo=lo, sp=sp: e.tensor_copy(
                out=sat_st[sp][:, :], in_=pst[:, lo:lo + 1024]),
                reads=breg, writes=[b_sat_st[sp]])
            p.dma("gpsimd", lambda e, h=h, half=half, sp=sp: e.dma_start(
                out=satscr[h * 128:(h + 1) * 128, half * 1024:(half + 1) * 1024],
                in_=sat_st[sp][:, :]),
                reads=[b_sat_st[sp]], writes=[b_satscr])

    mkA2 = roll("A")
    b_acc = mkA2("out_acc")
    mkC = roll("C")
    b_hact = mkC("hact")
    b_gbf = [mkC("gbf0"), mkC("gbf1")]
    b_ubf = [mkC("ubf0"), mkC("ubf1")]
    mkD = roll("D")
    b_dwbf = mkD("dwbf")
    mkE2 = roll("E")
    b_gw8 = [mkE2("gw80"), mkE2("gw81")]
    b_uw8 = [mkE2("uw80"), mkE2("uw81")]
    b_dw8 = [mkE2("dw80"), mkE2("dw81")]
    b_gsil = [mkE2("gsil0"), mkE2("gsil1")]
    b_gs = mkE2("gs")
    b_us = mkE2("us")
    p.dma("sync", lambda e: e.dma_start(out=gs_sb[:, :], in_=gs_d[:, :]),
          writes=[b_gs])
    p.dma("sync", lambda e: e.dma_start(out=us_sb[:, :], in_=us_d[:, :]),
          writes=[b_us])

    for g in range(NG):
        for il in range(GRP):
            it = g * GRP + il
            s = it % 2
            p.dma("sync", lambda e, it=it, s=s: e.dma_start(out=gw8[s][:, :],
                                                            in_=gw_d[it, :, :, :]),
                  writes=[b_gw8[s]])
            p.dma("sync", lambda e, it=it, s=s: e.dma_start(out=uw8[s][:, :],
                                                            in_=uw_d[it, :, :, :]),
                  writes=[b_uw8[s]])
            p.dma("sync", lambda e, it=it, s=s: e.dma_start(out=dw8[s][:, :],
                                                            in_=dw_d[it, :, :]),
                  writes=[b_dw8[s]])
            p.op("scalar", lambda e, s=s: e.activation(out=gbf[s][:, :],
                                                       in_=gw8[s][:, :],
                                                       func=AF.Copy, bias=-128.0),
                 reads=[b_gw8[s]], writes=[b_gbf[s]])
            p.op("scalar", lambda e, s=s: e.activation(out=ubf[s][:, :],
                                                       in_=uw8[s][:, :],
                                                       func=AF.Copy, bias=-128.0),
                 reads=[b_uw8[s]], writes=[b_ubf[s]])
            p.op("scalar", lambda e, s=s, il=il: e.activation(
                out=dwbf[:, il * 2048:(il + 1) * 2048], in_=dw8[s][:, :],
                func=AF.Copy, bias=-128.0),
                reads=[b_dw8[s]], writes=[b_dwbf])
            for n2 in range(2):
                for kc in range(KC):
                    p.op("tensor", lambda e, s=s, n2=n2, kc=kc: e.matmul(
                        psA[:, n2 * 512:(n2 + 1) * 512],
                        gbf[s][:, kc * 128:(kc + 1) * 128],
                        xn_ffn[:, kc * 1024 + n2 * 512:kc * 1024 + (n2 + 1) * 512],
                        start=(kc == 0), stop=(kc == KC - 1)),
                        reads=[b_gbf[s], b_xnffn], writes=[b_psA_lo])
            for n2 in range(2):
                for kc in range(KC):
                    p.op("tensor", lambda e, s=s, n2=n2, kc=kc: e.matmul(
                        psA[:, 1024 + n2 * 512:1024 + (n2 + 1) * 512],
                        ubf[s][:, kc * 128:(kc + 1) * 128],
                        xn_ffn[:, kc * 1024 + n2 * 512:kc * 1024 + (n2 + 1) * 512],
                        start=(kc == 0), stop=(kc == KC - 1)),
                        reads=[b_ubf[s], b_xnffn], writes=[b_psA_hi])
            p.op("scalar", lambda e, s=s, it=it: e.activation(
                out=gsil[s][:, :], in_=psA[:, 0:1024], func=AF.Silu,
                scale=gs_sb[:, it:it + 1]),
                reads=[b_psA_lo], scalar_reads=[b_gs], writes=[b_gsil[s]])
            p.op("vector", lambda e, s=s, it=it, il=il: e.scalar_tensor_tensor(
                out=hact[:, il * 1024:(il + 1) * 1024], in0=psA[:, 1024:2048],
                scalar=us_sb[:, it:it + 1], in1=gsil[s][:, :],
                op0=ALU.mult, op1=ALU.mult),
                reads=[b_psA_hi, b_gsil[s]], scalar_reads=[b_us], writes=[b_hact])
        for tt in range(NH):
            for n4 in range(4):
                breg = b_psBb[n4]
                for il in range(GRP):
                    p.op("tensor", lambda e, tt=tt, n4=n4, il=il: e.matmul(
                        psB[:, n4 * 512:(n4 + 1) * 512],
                        hact[:, il * 1024 + tt * 128:il * 1024 + (tt + 1) * 128],
                        dwbf[:, il * 2048 + n4 * 512:il * 2048 + (n4 + 1) * 512],
                        start=(il == 0), stop=(il == GRP - 1)),
                        reads=[b_hact, b_dwbf], writes=[breg])
            for n4 in range(4):
                if g == 0:
                    p.op("vector", lambda e, tt=tt, n4=n4: e.tensor_copy(
                        out=out_acc[:, tt * 2048 + n4 * 512:
                                    tt * 2048 + (n4 + 1) * 512],
                        in_=psB[:, n4 * 512:(n4 + 1) * 512]),
                        reads=[b_psBb[n4]], writes=[b_acc])
                else:
                    p.op("vector", lambda e, tt=tt, n4=n4: e.tensor_add(
                        out_acc[:, tt * 2048 + n4 * 512:
                                tt * 2048 + (n4 + 1) * 512],
                        out_acc[:, tt * 2048 + n4 * 512:
                                tt * 2048 + (n4 + 1) * 512],
                        psB[:, n4 * 512:(n4 + 1) * 512]),
                        reads=[b_psBb[n4], b_acc], writes=[b_acc])

    mkE3 = roll("E")
    b_xr = mkE3("xr")
    b_ta = mkE3("ta")
    b_tb = mkE3("tb")
    b_sat = [mkE3("sat0"), mkE3("sat1")]
    b_dsb = mkE3("dsb")
    p.dma("sync", lambda e: e.dma_start(out=dsb_sb[:, :], in_=dsb_d[:, :]),
          writes=[b_dsb])
    out_evs = []
    for h in range(NH):
        s = h % 2
        p.dma("sync", lambda e, h=h: e.dma_start(
            out=xr[:, :], in_=x_d[h * 128:(h + 1) * 128, :]),
            writes=[b_xr])
        p.dma("sync", lambda e, h=h, s=s: e.dma_start(
            out=sat[s][:, :], in_=satscr[h * 128:(h + 1) * 128, :]),
            reads=[b_satscr], writes=[b_sat[s]])
        p.op("vector", lambda e, h=h: e.tensor_mul(
            ta[:, :], out_acc[:, h * 2048:(h + 1) * 2048], dsb_sb[:, :]),
            reads=[b_acc, b_dsb], writes=[b_ta])
        p.op("scalar", lambda e: e.activation(out=tb[:, :], in_=ta[:, :],
                                              func=AF.Silu),
             reads=[b_ta], writes=[b_tb])
        p.op("vector", lambda e, s=s: e.tensor_mul(ta[:, :], tb[:, :], sat[s][:, :]),
             reads=[b_tb, b_sat[s]], writes=[b_ta])
        p.op("vector", lambda e: e.tensor_add(tb[:, :], ta[:, :], xr[:, :]),
             reads=[b_ta, b_xr], writes=[b_tb])
        ev = p.dma("scalar", lambda e, h=h: e.dma_start(
            out=out_d[h * 128:(h + 1) * 128, :], in_=tb[:, :]),
            reads=[b_tb])
        out_evs.append(ev)
    p.op("gpsimd", lambda e: e.engine_nop(), deps=out_evs)

    semvals, nwaits = p.finalize()
    nops = len(p.ops)
    with ExitStack() as stack:
        sems = {n: stack.enter_context(nc.semaphore("s_" + n)) for n in ENGINES}
        with nc.Block() as block:
            run = p.emit(sems)
            for n in ENGINES:
                getattr(block, n)((lambda nn: lambda e: run(e, nn))(n))
    return nc, nops, nwaits, semvals


def _prep_inputs(inputs):
    x = np.ascontiguousarray(inputs["L0_attn_x"])
    wq = inputs["L0_attn_wq"]
    wk = inputs["L0_attn_wk"]
    wv = inputs["L0_attn_wv"]
    wo = inputs["L0_attn_wo"]

    def u8view(a):
        a = np.ascontiguousarray(a.astype(np.int32, copy=False))
        return a.view(np.uint8).reshape(a.shape[0], a.shape[1], 4)[:, :, 0].copy()

    gw = u8view(inputs["L0_ffn_gate_w"])
    uw = u8view(inputs["L0_ffn_up_w"])
    dw = u8view(inputs["L0_ffn_down_w"])

    wq_r = wq.reshape(16, 128, 16, 128).transpose(0, 3, 2, 1)
    wk_r = wk.reshape(16, 128, 16, 128).transpose(0, 3, 2, 1)
    wv_r = wv.reshape(16, 128, 16, 128).transpose(0, 3, 2, 1)
    gw_r = np.ascontiguousarray(gw.reshape(64, 128, 16, 128).transpose(0, 3, 2, 1))
    uw_r = np.ascontiguousarray(uw.reshape(64, 128, 16, 128).transpose(0, 3, 2, 1))
    dw_r = np.ascontiguousarray(dw.T.reshape(64, 128, 2048))
    gs_r = np.ascontiguousarray(inputs["L0_ffn_gate_scale"].reshape(64, 128).T)
    us_r = np.ascontiguousarray(inputs["L0_ffn_up_scale"].reshape(64, 128).T)
    dsb = np.ascontiguousarray(
        np.broadcast_to(inputs["L0_ffn_down_scale"][None, :], (128, 2048)))
    lam4 = np.ascontiguousarray(np.stack(
        [inputs["L0_attn_lambda_q1"], inputs["L0_attn_lambda_q2"],
         inputs["L0_attn_lambda_k1"], inputs["L0_attn_lambda_k2"]], axis=1))

    perms, wo_par, wq_par, wk_par, wv2_par = [], [], [], [], []
    hd = np.arange(128)
    for parity in range(2):
        h0 = 8 * parity
        perm = np.empty(2048, np.int64)
        for half, hh0 in enumerate([h0, (h0 + 8) % 16]):
            for hl in range(8):
                perm[half * 1024 + hl * 128 + hd] = 16 * hd + hh0 + hl
        perms.append(perm)
        wop = np.ascontiguousarray(wo.T[perm])
        wo_par.append(np.ascontiguousarray(
            wop.reshape(16, 128, 2, 1024).transpose(2, 0, 1, 3)))
        wq_par.append(np.ascontiguousarray(wq_r[h0:h0 + 8]))
        wk_par.append(np.ascontiguousarray(wk_r[h0:h0 + 8]))
        wv2_par.append(np.ascontiguousarray(
            wv_r[h0:h0 + 8].reshape(4, 2, 128, 16, 128)
            .transpose(0, 2, 3, 1, 4).reshape(4, 128, 16, 256)))

    in_maps = []
    for core in range(8):
        b, parity = core // 2, core % 2
        x_perm = np.ascontiguousarray(x[b][perms[parity]])
        in_maps.append({
            "x": x_perm,
            "wq": wq_par[parity], "wk": wk_par[parity], "wv2": wv2_par[parity],
            "wo": wo_par[parity],
            "gw": gw_r, "uw": uw_r, "dw": dw_r,
            "gs": gs_r, "us": us_r, "dsb": dsb, "lam4": lam4,
        })
    return in_maps, perms


def kernel(**inputs):
    in_maps, perms = _prep_inputs(inputs)
    nc, nops, nwaits, semvals = build_nc()
    res = run_bass_kernel_spmd(nc, in_maps, core_ids=list(range(8)))
    out = np.empty((B, S, DM), dtype=np.float32)
    for core in range(8):
        b, parity = core // 2, core % 2
        r = res.results[core]["out"]
        out[b][perms[parity][:1024]] = r
    return out


# version 39
# speedup vs baseline: 1.0665x; 1.0665x over previous
import numpy as np
from contextlib import ExitStack
from collections import defaultdict

import concourse.bass as bass
import concourse.mybir as mybir
from concourse.bass_utils import run_bass_kernel_spmd

F32 = mybir.dt.float32
BF16 = mybir.dt.bfloat16
U8 = mybir.dt.uint8
AF = mybir.ActivationFunctionType
ALU = mybir.AluOpType
AX = mybir.AxisListType

ENGINES = ("sync", "gpsimd", "scalar", "vector", "tensor")


class Op:
    __slots__ = ("engine", "fn", "deps", "strict", "is_dma", "seq", "inc",
                 "semval_after", "waits")

    def __init__(self, engine, fn, deps, is_dma, seq):
        self.engine = engine
        self.fn = fn
        self.deps = deps
        self.strict = set()
        self.is_dma = is_dma
        self.seq = seq
        self.inc = 0
        self.semval_after = 0
        self.waits = []


class Buf:
    __slots__ = ("writers", "readers", "name")

    def __init__(self, name="buf", init_deps=()):
        self.name = name
        self.writers = list(init_deps)
        self.readers = []


class Prog:
    def __init__(self, nc):
        self.nc = nc
        self.ops = []

    def op(self, engine, fn, reads=(), writes=(), deps=(), scalar_reads=()):
        return self._add(engine, fn, reads, writes, deps, scalar_reads, False)

    def dma(self, engine, fn, reads=(), writes=(), deps=()):
        return self._add(engine, fn, reads, writes, deps, (), True)

    def _add(self, engine, fn, reads, writes, deps, scalar_reads, is_dma):
        d = list(deps)
        strict = set()
        for b in reads:
            d.extend(b.writers)
        for b in scalar_reads:
            d.extend(b.writers)
            strict.update(w.seq for w in b.writers)
        wdeps = []
        for b in writes:
            wdeps.extend(b.readers)
            wdeps.extend(b.writers)
        o = Op(engine, fn, d + wdeps, is_dma, len(self.ops))
        o.strict = strict
        self.ops.append(o)
        for b in reads:
            b.readers.append(o)
        for b in scalar_reads:
            b.readers.append(o)
        for b in writes:
            b.writers = [o]
            b.readers = []
        return o

    def finalize(self):
        for o in self.ops:
            if o.is_dma:
                o.inc = 16
            for d in o.deps:
                if d.inc:
                    continue
                if d.is_dma or d.engine != o.engine or d.seq in o.strict:
                    d.inc = 16 if d.is_dma else 1
        semval = defaultdict(int)
        for o in self.ops:
            semval[o.engine] += o.inc
            o.semval_after = semval[o.engine]
        for o in self.ops:
            need = {}
            for d in o.deps:
                if (d.engine == o.engine and not d.is_dma
                        and d.seq not in o.strict):
                    continue
                if d.engine == o.engine and d.is_dma and o.is_dma:
                    continue
                if d.semval_after > need.get(d.engine, 0):
                    need[d.engine] = d.semval_after
            o.waits = sorted(need.items())
        maxseen = defaultdict(lambda: defaultdict(int))
        nwaits = 0
        for o in self.ops:
            seen = maxseen[o.engine]
            w = []
            for e, v in o.waits:
                if v > seen[e]:
                    w.append((e, v))
                    seen[e] = v
            o.waits = w
            nwaits += len(w)
        return dict(semval), nwaits

    def emit(self, sems):
        byeng = defaultdict(list)
        for o in self.ops:
            byeng[o.engine].append(o)

        def run(eng_obj, engine_name):
            for o in byeng[engine_name]:
                for e, v in o.waits:
                    eng_obj.wait_ge(sems[e], v)
                ins = o.fn(eng_obj)
                if o.inc:
                    ins.then_inc(sems[o.engine], o.inc)
        return run


B, S, DM = 4, 2048, 2048
H, HD, NH = 16, 128, 8
KC = 16
NQT = 16
NIT = 64
GRP = 8
NG = NIT // GRP
EXP_SHIFT = -12.0
QK_SCALE = 1.0 / float(np.sqrt(128.0))
RMS_EPS = 1.1920928955078125e-07
LN_EPS = 1e-5
LAM_INIT = 0.8
RMS_GAIN_BIAS = float(np.log(0.2 * np.sqrt(128.0)))

REG_A = 16512
REG_B = REG_A + 65536
REG_C = REG_B + 32768
REG_D = REG_C + 32768
REG_E = REG_D + 32768
PERSIST = 226304
SBUF_TOP = 229344


def build_nc():
    nc = bass.Bass("TRN2", target_bir_lowering=False)

    x_d = nc.dram_tensor("x", [S, DM], F32, kind="ExternalInput")
    wq_d = nc.dram_tensor("wq", [NH, 128, KC, 128], F32, kind="ExternalInput")
    wk_d = nc.dram_tensor("wk", [NH, 128, KC, 128], F32, kind="ExternalInput")
    wv2_d = nc.dram_tensor("wv2", [NH // 2, 128, KC, 256], F32, kind="ExternalInput")
    wo_d = nc.dram_tensor("wo", [2, KC, 128, 1024], F32, kind="ExternalInput")
    gw_d = nc.dram_tensor("gw", [NIT, 128, KC, 128], U8, kind="ExternalInput")
    uw_d = nc.dram_tensor("uw", [NIT, 128, KC, 128], U8, kind="ExternalInput")
    dw_d = nc.dram_tensor("dw", [NIT, 128, DM], U8, kind="ExternalInput")
    gs_d = nc.dram_tensor("gs", [128, NIT], F32, kind="ExternalInput")
    us_d = nc.dram_tensor("us", [128, NIT], F32, kind="ExternalInput")
    dsb_d = nc.dram_tensor("dsb", [128, DM], F32, kind="ExternalInput")
    lam_d = nc.dram_tensor("lam4", [128, 4], F32, kind="ExternalInput")
    satscr = nc.dram_tensor("satscr", [NH * 128, DM], BF16, kind="Internal")
    out_d = nc.dram_tensor("out", [NH * 128, DM], F32, kind="ExternalOutput")

    at = nc.alloc_sbuf_tensor_at
    xn_fm = at("xn_fm", [128, KC * 2048], BF16, offset=REG_A)
    wo_h = [at("wo_h0", [128, KC * 1024], BF16, offset=REG_A),
            at("wo_h1", [128, KC * 1024], BF16, offset=REG_A + 32768)]
    out_acc = at("out_acc", [128, NH * 2048], F32, offset=REG_A)
    xn_ffn = at("xn_ffn", [128, KC * 1024], BF16, offset=REG_B)
    normed = at("normed", [128, NH * NQT * 128], BF16, offset=REG_C)
    hact = at("hact", [128, GRP * 1024], BF16, offset=REG_C)
    gbf = [at("gbf0", [128, 2048], BF16, offset=REG_C + 16384),
           at("gbf1", [128, 2048], BF16, offset=REG_C + 20480)]
    ubf = [at("ubf0", [128, 2048], BF16, offset=REG_C + 24576),
           at("ubf1", [128, 2048], BF16, offset=REG_C + 28672)]
    sat_st = [at("sat_st0", [128, 1024], BF16, offset=REG_D),
              at("sat_st1", [128, 1024], BF16, offset=REG_D + 2048)]
    dwbf = at("dwbf", [128, GRP * 2048], BF16, offset=REG_D)
    xt = [at("xt0", [128, 2048], F32, offset=REG_E),
          at("xt1", [128, 2048], F32, offset=REG_E + 8192)]
    xnt = [at("xnt0", [128, 2048], BF16, offset=REG_E + 16384),
           at("xnt1", [128, 2048], BF16, offset=REG_E + 20480)]
    junk2k = at("junk2k", [128, 2048], BF16, offset=REG_E + 24576)
    wsA = at("wsA", [128, 2048], BF16, offset=REG_E)
    wsB = at("wsB", [128, 2048], BF16, offset=REG_E + 4096)
    v2 = at("v2", [128, 4096], BF16, offset=REG_E + 8192)
    wv2s = at("wv2s", [128, 4096], BF16, offset=REG_E + 16384)
    qfm = at("qfm", [128, 2048], BF16, offset=REG_E + 16384)
    kfm = at("kfm", [128, 2048], BF16, offset=REG_E + 20480)
    e1r = [at("e1_0", [128, 2048], BF16, offset=REG_E + 24576),
           at("e1_1", [128, 2048], BF16, offset=REG_E + 40960)]
    e2r = [at("e2_0", [128, 2048], BF16, offset=REG_E + 28672),
           at("e2_1", [128, 2048], BF16, offset=REG_E + 32768),
           at("e2_2", [128, 2048], BF16, offset=REG_E + 36864)]
    aTa = at("aTa", [128, 1024], BF16, offset=REG_D + 4096)
    aTb = at("aTb", [128, 1024], BF16, offset=REG_D + 6144)
    gw8 = [at("gw80", [128, 2048], U8, offset=REG_E),
           at("gw81", [128, 2048], U8, offset=REG_E + 2048)]
    uw8 = [at("uw80", [128, 2048], U8, offset=REG_E + 4096),
           at("uw81", [128, 2048], U8, offset=REG_E + 6144)]
    dw8 = [at("dw80", [128, 2048], U8, offset=REG_E + 8192),
           at("dw81", [128, 2048], U8, offset=REG_E + 10240)]
    gsil = [at("gsil0", [128, 1024], BF16, offset=REG_E + 12288),
            at("gsil1", [128, 1024], BF16, offset=REG_E + 14336)]
    gs_sb = at("gs_sb", [128, NIT], F32, offset=REG_E + 16384)
    us_sb = at("us_sb", [128, NIT], F32, offset=REG_E + 16640)
    xr = at("xr", [128, 2048], F32, offset=REG_E)
    ta = at("ta", [128, 2048], F32, offset=REG_E + 8192)
    tb = at("tb", [128, 2048], F32, offset=REG_E + 16384)
    sat = [at("sat0", [128, 2048], BF16, offset=REG_E + 24576),
           at("sat1", [128, 2048], BF16, offset=REG_E + 28672)]
    dsb_sb = at("dsb_sb", [128, 2048], F32, offset=REG_E + 32768)
    stats = at("stats", [128, 64], F32, offset=PERSIST)
    ones = at("ones", [1, 128], F32, offset=PERSIST + 256)
    lam_sb = at("lam_sb", [128, 4], F32, offset=PERSIST + 768)
    id_sb = at("id_sb", [128, 128], BF16, offset=PERSIST + 800)
    junk1 = at("junk1", [128, 128], BF16, offset=PERSIST + 1056)
    cellc = at("cellc", [128, 4], F32, offset=PERSIST + 1312)
    assert PERSIST + 1344 <= SBUF_TOP

    psA = nc.alloc_psum_tensor("psA", [128, 2048], F32)
    psB = nc.alloc_psum_tensor("psB", [128, 2048], F32)
    psA_bfv = psA[:, 0:1024].bitcast(BF16)
    vb4 = psB[:, 0:512].bitcast(BF16)
    vb5 = psB[:, 512:1024].bitcast(BF16)
    AVC = 1024

    p = Prog(nc)
    C_SUM, C_SQ, C_MEAN, C_M2, C_MSQ, C_VAR = 0, 1, 2, 3, 4, 5
    C_LNV, C_RSTD = 6, 7
    Q_S1A, Q_S1B, Q_S2A, Q_S2B, Q_S1, Q_S2, Q_R2, Q_T, Q_NEGC = range(9)
    Q_T1, Q_DD, Q_LND, Q_SCQ = 9, 10, 11, 12
    C_ED1, C_ED2, C_LAM, C_NLAM1, C_NLAM = 56, 57, 58, 59, 60

    def scol(c):
        return stats[:, c:c + 1]

    regions = defaultdict(list)

    def newbuf(name, region=None):
        b = Buf(name)
        if region is not None:
            regions[region].append(b)
        return b

    def roll(region):
        deps = []
        for b in regions[region]:
            deps.extend(b.writers)
            deps.extend(b.readers)
        regions[region] = []

        def make(name):
            b = Buf(name, init_deps=list(deps))
            regions[region].append(b)
            return b
        return make

    b_x = [newbuf("xt0", "E"), newbuf("xt1", "E")]
    b_xnt = [newbuf("xnt0", "E"), newbuf("xnt1", "E")]
    b_junk2k = newbuf("junk2k", "E")
    b_junk1 = newbuf("junk1")
    b_stats = {c: newbuf(f"st{c}") for c in range(64)}
    b_ones = newbuf("ones")
    b_lam = newbuf("lam")
    b_id = newbuf("id")
    b_xn = newbuf("xn_fm", "A")
    b_xnffn = newbuf("xn_ffn")
    b_normed = newbuf("normed", "C")
    b_psA_lo = newbuf("psA_lo")
    b_psA_hi = newbuf("psA_hi")
    b_psBb = [newbuf(f"psB{i}") for i in range(4)]
    b_psB_lo = [b_psBb[0], b_psBb[1]]
    b_psB_hi = [b_psBb[2], b_psBb[3]]
    PSQ = [[b_psA_lo], [b_psA_hi], b_psB_lo, b_psB_hi]

    p.dma("sync", lambda e: e.dma_start(out=lam_sb[:, :], in_=lam_d[:, :]),
          writes=[b_lam])
    b_cellc = newbuf("cellc")
    p.op("gpsimd", lambda e: e.memset(cellc[:, 0:1], LN_EPS), writes=[b_cellc])
    p.op("gpsimd", lambda e: e.memset(cellc[:, 1:2], EXP_SHIFT),
         reads=[b_cellc], writes=[b_cellc])
    p.op("gpsimd", lambda e: e.memset(cellc[:, 2:3], RMS_GAIN_BIAS),
         reads=[b_cellc], writes=[b_cellc])
    p.op("gpsimd", lambda e: e.memset(ones[:, :], 1.0), writes=[b_ones])
    p.op("gpsimd", lambda e: e.memset(id_sb[:, :], 0.0), writes=[b_id])
    p.op("gpsimd", lambda e: e.affine_select(
        out=id_sb[:, :], in_=id_sb[:, :], compare_op=ALU.not_equal,
        fill=1.0, base=0, pattern=[[-1, 128]], channel_multiplier=1),
        reads=[b_id], writes=[b_id])
    p.op("tensor", lambda e: e.matmul(psA[0:1, 0:1], lam_sb[:, 0:1], lam_sb[:, 2:3],
                                      start=True, stop=True),
         reads=[b_lam], writes=[b_psA_lo])
    p.op("tensor", lambda e: e.matmul(psA[0:1, 1:2], lam_sb[:, 1:2], lam_sb[:, 3:4],
                                      start=True, stop=True),
         reads=[b_lam], writes=[b_psA_lo])
    p.op("scalar", lambda e: e.activation(out=stats[0:1, C_ED1:C_ED1 + 1],
                                          in_=psA[0:1, 0:1], func=AF.Exp),
         reads=[b_psA_lo], writes=[b_stats[C_ED1]])
    p.op("scalar", lambda e: e.activation(out=stats[0:1, C_ED2:C_ED2 + 1],
                                          in_=psA[0:1, 1:2], func=AF.Exp),
         reads=[b_psA_lo], writes=[b_stats[C_ED2]])
    p.op("vector", lambda e: e.tensor_sub(stats[0:1, C_LAM:C_LAM + 1],
                                          stats[0:1, C_ED1:C_ED1 + 1],
                                          stats[0:1, C_ED2:C_ED2 + 1]),
         scalar_reads=[b_stats[C_ED1], b_stats[C_ED2]], writes=[b_stats[C_LAM]])
    p.op("vector", lambda e: e.tensor_scalar(
        out=stats[0:1, C_NLAM1:C_NLAM1 + 1], in0=stats[0:1, C_LAM:C_LAM + 1],
        scalar1=LAM_INIT, scalar2=-1.0, op0=ALU.add, op1=ALU.mult),
        scalar_reads=[b_stats[C_LAM]], writes=[b_stats[C_NLAM1]])
    p.op("tensor", lambda e: e.matmul(psB[:, 0:1], ones[:, :],
                                      stats[0:1, C_NLAM1:C_NLAM1 + 1],
                                      start=True, stop=True),
         reads=[b_ones], scalar_reads=[b_stats[C_NLAM1]], writes=[b_psBb[0]])
    p.op("vector", lambda e: e.tensor_copy(out=scol(C_NLAM), in_=psB[:, 0:1]),
         reads=[b_psBb[0]], writes=[b_stats[C_NLAM]])

    for t in range(KC):
        s = t % 2
        bx, bn = b_x[s], b_xnt[s]
        p.dma("sync", lambda e, t=t, s=s: e.dma_start(
            out=xt[s][:, :], in_=x_d[t * 128:(t + 1) * 128, :]), writes=[bx])
        p.op("vector", lambda e, s=s: e.reduce_sum(scol(C_SUM), xt[s][:, :], axis=AX.X),
             reads=[bx], writes=[b_stats[C_SUM]])
        p.op("scalar", lambda e, s=s: e.activation(
            out=junk2k[:, :], in_=xt[s][:, :], func=AF.Square,
            accum_out=scol(C_SQ)),
            reads=[bx], writes=[b_junk2k, b_stats[C_SQ]])
        p.op("vector", lambda e: e.tensor_scalar_mul(scol(C_MEAN), scol(C_SUM),
                                                     1.0 / DM),
             scalar_reads=[b_stats[C_SUM]], writes=[b_stats[C_MEAN]])
        p.op("vector", lambda e: e.tensor_scalar_mul(scol(C_M2), scol(C_SQ),
                                                     1.0 / DM),
             scalar_reads=[b_stats[C_SQ]], writes=[b_stats[C_M2]])
        p.op("vector", lambda e: e.tensor_mul(scol(C_MSQ), scol(C_MEAN), scol(C_MEAN)),
             scalar_reads=[b_stats[C_MEAN]], writes=[b_stats[C_MSQ]])
        p.op("vector", lambda e: e.tensor_sub(scol(C_VAR), scol(C_M2), scol(C_MSQ)),
             scalar_reads=[b_stats[C_M2], b_stats[C_MSQ]], writes=[b_stats[C_VAR]])
        p.op("scalar", lambda e: e.activation(out=scol(C_LNV), in_=scol(C_VAR),
                                              func=AF.Ln, bias=cellc[:, 0:1]),
             scalar_reads=[b_stats[C_VAR], b_cellc],
             writes=[b_stats[C_LNV]])
        p.op("scalar", lambda e: e.activation(out=scol(C_RSTD), in_=scol(C_LNV),
                                              func=AF.Exp, scale=-0.5),
             reads=[b_stats[C_LNV]], writes=[b_stats[C_RSTD]])
        p.op("vector", lambda e, s=s: e.tensor_scalar(
            out=xnt[s][:, :], in0=xt[s][:, :], scalar1=scol(C_MEAN),
            scalar2=scol(C_RSTD), op0=ALU.subtract, op1=ALU.mult),
            reads=[bx], scalar_reads=[b_stats[C_MEAN], b_stats[C_RSTD]],
            writes=[bn])
        for c in range(KC):
            p.op("tensor", lambda e, s=s, c=c: e.transpose(
                psA_bfv[:, c * 128:(c + 1) * 128],
                xnt[s][:, c * 128:(c + 1) * 128], id_sb[:, :]),
                reads=[bn, b_id], writes=[b_psA_lo])
        p.op("vector", lambda e, t=t: e.tensor_copy(
            out=xn_fm[:, :].rearrange("p (c n) -> p c n", n=2048)
            [:, :, t * 128:(t + 1) * 128],
            in_=psA_bfv[:, :]),
            reads=[b_psA_lo], writes=[b_xn])
    for c in range(KC):
        p.op("vector", lambda e, c=c: e.tensor_copy(
            out=xn_ffn[:, c * 1024:(c + 1) * 1024],
            in_=xn_fm[:, c * 2048:c * 2048 + 1024]),
            reads=[b_xn], writes=[b_xnffn])

    mkE = roll("E")
    b_wsA = mkE("wsA")
    b_wsB = mkE("wsB")
    b_wv2 = mkE("wv2")
    b_v2 = mkE("v2")
    b_qfm = mkE("qfm")
    b_kfm = mkE("kfm")
    b_e1r = [mkE("e1_0"), mkE("e1_1")]
    b_e2r = [[mkE(f"e2_{r}_{kh}") for kh in range(2)] for r in range(3)]
    b_aTa = newbuf("aTa", "D")
    b_aTb = newbuf("aTb", "D")

    for h in range(NH):
        par = h % 2
        p.dma("gpsimd", lambda e, h=h: e.dma_start(out=wsA[:, :],
                                                   in_=wq_d[h, :, :, :]),
              writes=[b_wsA])
        p.dma("gpsimd", lambda e, h=h: e.dma_start(out=wsB[:, :],
                                                   in_=wk_d[h, :, :, :]),
              writes=[b_wsB])
        if par == 0:
            p.dma("gpsimd", lambda e, h=h: e.dma_start(
                out=wv2s[:, :], in_=wv2_d[h // 2, :, :, :]),
                writes=[b_wv2, b_qfm, b_kfm])
            for half in (0, 1):
                pst = psA if half == 0 else psB
                for tt in range(8):
                    ttg = half * 8 + tt
                    breg = ((b_psA_lo if tt < 4 else b_psA_hi) if half == 0
                            else b_psBb[tt // 2])
                    for kc in range(KC):
                        p.op("tensor", lambda e, pst=pst, tt=tt, ttg=ttg, kc=kc:
                             e.matmul(pst[:, tt * 256:(tt + 1) * 256],
                                      xn_fm[:, kc * 2048 + ttg * 128:
                                            kc * 2048 + (ttg + 1) * 128],
                                      wv2s[:, kc * 256:(kc + 1) * 256],
                                      start=(kc == 0), stop=(kc == KC - 1)),
                             reads=[b_xn, b_wv2], writes=[breg])
            p.op("vector", lambda e: e.tensor_copy(out=v2[:, 0:2048], in_=psA[:, :]),
                 reads=[b_psA_lo, b_psA_hi], writes=[b_v2])
            p.op("vector", lambda e: e.tensor_copy(out=v2[:, 2048:4096], in_=psB[:, :]),
                 reads=b_psBb, writes=[b_v2])
        for n in range(4):
            breg = b_psA_lo if n < 2 else b_psA_hi
            for kc in range(KC):
                p.op("tensor", lambda e, n=n, kc=kc: e.matmul(
                    psA[:, n * 512:(n + 1) * 512],
                    wsA[:, kc * 128:(kc + 1) * 128],
                    xn_fm[:, kc * 2048 + n * 512:kc * 2048 + (n + 1) * 512],
                    start=(kc == 0), stop=(kc == KC - 1)),
                    reads=[b_wsA, b_xn], writes=[breg])
        p.op("vector", lambda e: e.tensor_scalar_mul(qfm[:, :], psA[:, :], QK_SCALE),
             reads=[b_psA_lo, b_psA_hi], writes=[b_qfm, b_wv2])
        for n in range(4):
            breg = b_psBb[n]
            for kc in range(KC):
                p.op("tensor", lambda e, n=n, kc=kc: e.matmul(
                    psB[:, n * 512:(n + 1) * 512],
                    wsB[:, kc * 128:(kc + 1) * 128],
                    xn_fm[:, kc * 2048 + n * 512:kc * 2048 + (n + 1) * 512],
                    start=(kc == 0), stop=(kc == KC - 1)),
                    reads=[b_wsB, b_xn], writes=[breg])
        p.op("vector", lambda e: e.tensor_copy(out=kfm[:, :], in_=psB[:, :]),
             reads=b_psBb, writes=[b_kfm, b_wv2])

        def emit_scores_half(qt, kh):
            rot = qt % 3
            cb = 8 + 13 * rot
            for n in range(2):
                p.op("tensor", lambda e, qt=qt, kh=kh, n=n: e.matmul(
                    psA[:, n * 512:(n + 1) * 512],
                    qfm[0:64, qt * 128:(qt + 1) * 128],
                    kfm[0:64, kh * 1024 + n * 512:kh * 1024 + (n + 1) * 512],
                    start=True, stop=True),
                    reads=[b_qfm, b_kfm], writes=[b_psA_lo])
            for n in range(2):
                p.op("tensor", lambda e, qt=qt, kh=kh, n=n: e.matmul(
                    psA[:, 1024 + n * 512:1024 + (n + 1) * 512],
                    qfm[64:128, qt * 128:(qt + 1) * 128],
                    kfm[64:128, kh * 1024 + n * 512:kh * 1024 + (n + 1) * 512],
                    start=True, stop=True),
                    reads=[b_qfm, b_kfm], writes=[b_psA_hi])
            c1 = cb + (Q_S1A if kh == 0 else Q_S1B)
            c2 = cb + (Q_S2A if kh == 0 else Q_S2B)
            pe1 = qt % 2
            p.op("scalar", lambda e, kh=kh, c1=c1, pe1=pe1: e.activation(
                out=e1r[pe1][:, kh * 1024:(kh + 1) * 1024], in_=psA[:, 0:1024],
                func=AF.Exp, bias=cellc[:, 1:2], accum_out=scol(c1)),
                reads=[b_psA_lo], scalar_reads=[b_cellc],
                writes=[b_e1r[pe1], b_stats[c1]])
            p.op("scalar", lambda e, kh=kh, rot=rot, c2=c2: e.activation(
                out=e2r[rot][:, kh * 1024:(kh + 1) * 1024], in_=psA[:, 1024:2048],
                func=AF.Exp, bias=cellc[:, 1:2], accum_out=scol(c2)),
                reads=[b_psA_hi], scalar_reads=[b_cellc],
                writes=[b_e2r[rot][kh], b_stats[c2]])

        def emit_combine(qt):
            rot = qt % 3
            cb = 8 + 13 * rot
            p.op("vector", lambda e, cb=cb: e.tensor_add(
                scol(cb + Q_S1), scol(cb + Q_S1A), scol(cb + Q_S1B)),
                scalar_reads=[b_stats[cb + Q_S1A], b_stats[cb + Q_S1B]],
                writes=[b_stats[cb + Q_S1]])
            p.op("vector", lambda e, cb=cb: e.tensor_add(
                scol(cb + Q_S2), scol(cb + Q_S2A), scol(cb + Q_S2B)),
                scalar_reads=[b_stats[cb + Q_S2A], b_stats[cb + Q_S2B]],
                writes=[b_stats[cb + Q_S2]])
            p.op("vector", lambda e, cb=cb: e.reciprocal(scol(cb + Q_R2),
                                                         scol(cb + Q_S2)),
                 scalar_reads=[b_stats[cb + Q_S2]], writes=[b_stats[cb + Q_R2]])
            p.op("vector", lambda e, cb=cb: e.tensor_mul(
                scol(cb + Q_T), scol(cb + Q_S1), scol(cb + Q_R2)),
                scalar_reads=[b_stats[cb + Q_S1], b_stats[cb + Q_R2]],
                writes=[b_stats[cb + Q_T]])
            p.op("vector", lambda e, cb=cb: e.tensor_mul(
                scol(cb + Q_NEGC), scol(cb + Q_T), scol(C_NLAM)),
                scalar_reads=[b_stats[cb + Q_T], b_stats[C_NLAM]],
                writes=[b_stats[cb + Q_NEGC]])
            pe1 = qt % 2
            for kh in range(2):
                p.op("vector", lambda e, rot=rot, kh=kh, cb=cb:
                     e.tensor_scalar_mul(
                        e2r[rot][:, kh * 1024:(kh + 1) * 1024],
                        e2r[rot][:, kh * 1024:(kh + 1) * 1024],
                        scol(cb + Q_NEGC)),
                     reads=[b_e2r[rot][kh]],
                     scalar_reads=[b_stats[cb + Q_NEGC]],
                     writes=[b_e2r[rot][kh]])
                p.op("vector", lambda e, rot=rot, kh=kh, pe1=pe1:
                     e.tensor_add(
                        e2r[rot][:, kh * 1024:(kh + 1) * 1024],
                        e2r[rot][:, kh * 1024:(kh + 1) * 1024],
                        e1r[pe1][:, kh * 1024:(kh + 1) * 1024]),
                     reads=[b_e2r[rot][kh], b_e1r[pe1]],
                     writes=[b_e2r[rot][kh]])

        def emit_tail_a(qt, h, par):
            rot = qt % 3
            a = e2r[rot]
            ba = b_e2r[rot][0]
            for kt in range(8):
                p.op("tensor", lambda e, kt=kt, a=a, ba=ba: e.transpose(
                    vb4[:, kt * 128:(kt + 1) * 128],
                    a[:, kt * 128:(kt + 1) * 128], id_sb[:, :]),
                    reads=[ba, b_id], writes=[b_psBb[0]])
            p.op("vector", lambda e: e.tensor_copy(
                out=aTa[:, :], in_=vb4[:, :]),
                reads=[b_psBb[0]], writes=[b_aTa])

        def emit_tail_av_a(qt, h, par):
            avc = AVC + 512 * (qt % 2)
            bav = b_psBb[2 + qt % 2]
            for kt in range(8):
                p.op("tensor", lambda e, kt=kt, par=par, avc=avc: e.matmul(
                    psB[:, avc:avc + 128],
                    aTa[:, kt * 128:(kt + 1) * 128],
                    v2[:, kt * 256 + par * 128:kt * 256 + par * 128 + 128],
                    start=(kt == 0), stop=False),
                    reads=[b_aTa, b_v2], writes=[bav])

        def emit_tail_b(qt, h, par):
            rot = qt % 3
            cb = 8 + 13 * rot
            avc = AVC + 512 * (qt % 2)
            bav = b_psBb[2 + qt % 2]
            a = e2r[rot]
            ba = b_e2r[rot][1]
            for kt in range(8):
                ktg = 8 + kt
                p.op("tensor", lambda e, kt=kt, ktg=ktg, a=a, ba=ba: e.transpose(
                    vb5[:, kt * 128:(kt + 1) * 128],
                    a[:, ktg * 128:(ktg + 1) * 128], id_sb[:, :]),
                    reads=[ba, b_id], writes=[b_psBb[1]])
            p.op("vector", lambda e: e.tensor_copy(
                out=aTb[:, :], in_=vb5[:, :]),
                reads=[b_psBb[1]], writes=[b_aTb])
            for kt in range(8):
                ktg = 8 + kt
                p.op("tensor", lambda e, kt=kt, ktg=ktg, par=par, avc=avc: e.matmul(
                    psB[:, avc:avc + 128],
                    aTb[:, kt * 128:(kt + 1) * 128],
                    v2[:, ktg * 256 + par * 128:ktg * 256 + par * 128 + 128],
                    start=False, stop=(ktg == 15)),
                    reads=[b_aTb, b_v2], writes=[bav])
            p.op("scalar", lambda e, cb=cb, avc=avc: e.activation(
                out=junk1[:, :], in_=psB[:, avc:avc + 128], func=AF.Square,
                accum_out=scol(cb + Q_T1)),
                reads=[bav], writes=[b_junk1, b_stats[cb + Q_T1]])
            p.op("vector", lambda e, cb=cb: e.tensor_mul(
                scol(cb + Q_DD), scol(cb + Q_S1), scol(cb + Q_S1)),
                scalar_reads=[b_stats[cb + Q_S1]], writes=[b_stats[cb + Q_DD]])
            p.op("vector", lambda e, cb=cb: e.scalar_tensor_tensor(
                out=scol(cb + Q_DD), in0=scol(cb + Q_DD), scalar=128.0 * RMS_EPS,
                in1=scol(cb + Q_T1), op0=ALU.mult, op1=ALU.add),
                scalar_reads=[b_stats[cb + Q_DD], b_stats[cb + Q_T1]],
                writes=[b_stats[cb + Q_DD]])
            p.op("scalar", lambda e, cb=cb: e.activation(
                out=scol(cb + Q_LND), in_=scol(cb + Q_DD), func=AF.Ln),
                scalar_reads=[b_stats[cb + Q_DD]], writes=[b_stats[cb + Q_LND]])
            p.op("scalar", lambda e, cb=cb: e.activation(
                out=scol(cb + Q_SCQ), in_=scol(cb + Q_LND), func=AF.Exp,
                scale=-0.5, bias=cellc[:, 2:3]),
                reads=[b_stats[cb + Q_LND]], scalar_reads=[b_cellc],
                writes=[b_stats[cb + Q_SCQ]])
            p.op("vector", lambda e, h=h, qt=qt, cb=cb, avc=avc: e.tensor_scalar_mul(
                normed[:, (h * NQT + qt) * 128:(h * NQT + qt) * 128 + 128],
                psB[:, avc:avc + 128], scol(cb + Q_SCQ)),
                reads=[bav], scalar_reads=[b_stats[cb + Q_SCQ]],
                writes=[b_normed])

        def fill(n, qt):
            avc = AVC + 512 * ((qt + 1) % 2)
            bav = b_psBb[2 + (qt + 1) % 2]
            for _ in range(n):
                p.op("tensor", lambda e, avc=avc: e.matmul(
                    psB[:, avc:avc + 128], id_sb[:, :], id_sb[:, :],
                    start=True, stop=True),
                    reads=[b_id], writes=[bav])

        LAG = 2
        for qt in range(NQT + LAG):
            if qt < NQT:
                emit_scores_half(qt, 0)
            fill(9, qt)
            if qt >= LAG:
                emit_tail_a(qt - LAG, h, par)
            if qt < NQT:
                emit_scores_half(qt, 1)
            fill(9, qt)
            if qt >= LAG:
                emit_tail_av_a(qt - LAG, h, par)
                emit_tail_b(qt - LAG, h, par)
            if qt < NQT:
                emit_combine(qt)
            fill(6, qt)

    mkA = roll("A")
    b_woh = [mkA("wo_h0"), mkA("wo_h1")]
    b_sat_st = [newbuf("sat_st0", "D"), newbuf("sat_st1", "D")]
    b_satscr = newbuf("satscr")
    for half in range(2):
        for tc in range(KC):
            p.dma("gpsimd", lambda e, half=half, tc=tc: e.dma_start(
                out=wo_h[half][:, tc * 1024:(tc + 1) * 1024],
                in_=wo_d[half, tc, :, :]),
                writes=[b_woh[half]])
    for half in range(2):
        for h in range(NH):
            quad = (half * NH + h) % 4
            pst = psA if quad < 2 else psB
            lo = (quad % 2) * 1024
            breg = PSQ[quad]
            for n2 in range(2):
                for tc in range(KC):
                    p.op("tensor", lambda e, pst=pst, lo=lo, n2=n2, tc=tc, h=h, half=half:
                         e.matmul(pst[:, lo + n2 * 512:lo + (n2 + 1) * 512],
                                  normed[:, (h * NQT + tc) * 128:(h * NQT + tc) * 128 + 128],
                                  wo_h[half][:, tc * 1024 + n2 * 512:tc * 1024 + (n2 + 1) * 512],
                                  start=(tc == 0), stop=(tc == KC - 1)),
                         reads=[b_normed, b_woh[half]], writes=breg)
            sp = (half * NH + h) % 2
            p.op("vector", lambda e, pst=pst, lo=lo, sp=sp: e.tensor_copy(
                out=sat_st[sp][:, :], in_=pst[:, lo:lo + 1024]),
                reads=breg, writes=[b_sat_st[sp]])
            p.dma("gpsimd", lambda e, h=h, half=half, sp=sp: e.dma_start(
                out=satscr[h * 128:(h + 1) * 128, half * 1024:(half + 1) * 1024],
                in_=sat_st[sp][:, :]),
                reads=[b_sat_st[sp]], writes=[b_satscr])

    mkA2 = roll("A")
    b_acc = mkA2("out_acc")
    mkC = roll("C")
    b_hact = mkC("hact")
    b_gbf = [mkC("gbf0"), mkC("gbf1")]
    b_ubf = [mkC("ubf0"), mkC("ubf1")]
    mkD = roll("D")
    b_dwbf = mkD("dwbf")
    mkE2 = roll("E")
    b_gw8 = [mkE2("gw80"), mkE2("gw81")]
    b_uw8 = [mkE2("uw80"), mkE2("uw81")]
    b_dw8 = [mkE2("dw80"), mkE2("dw81")]
    b_gsil = [mkE2("gsil0"), mkE2("gsil1")]
    b_gs = mkE2("gs")
    b_us = mkE2("us")
    p.dma("sync", lambda e: e.dma_start(out=gs_sb[:, :], in_=gs_d[:, :]),
          writes=[b_gs])
    p.dma("sync", lambda e: e.dma_start(out=us_sb[:, :], in_=us_d[:, :]),
          writes=[b_us])

    for g in range(NG):
        for il in range(GRP):
            it = g * GRP + il
            s = it % 2
            p.dma("sync", lambda e, it=it, s=s: e.dma_start(out=gw8[s][:, :],
                                                            in_=gw_d[it, :, :, :]),
                  writes=[b_gw8[s]])
            p.dma("sync", lambda e, it=it, s=s: e.dma_start(out=uw8[s][:, :],
                                                            in_=uw_d[it, :, :, :]),
                  writes=[b_uw8[s]])
            p.dma("sync", lambda e, it=it, s=s: e.dma_start(out=dw8[s][:, :],
                                                            in_=dw_d[it, :, :]),
                  writes=[b_dw8[s]])
            p.op("scalar", lambda e, s=s: e.activation(out=gbf[s][:, :],
                                                       in_=gw8[s][:, :],
                                                       func=AF.Copy, bias=-128.0),
                 reads=[b_gw8[s]], writes=[b_gbf[s]])
            p.op("scalar", lambda e, s=s: e.activation(out=ubf[s][:, :],
                                                       in_=uw8[s][:, :],
                                                       func=AF.Copy, bias=-128.0),
                 reads=[b_uw8[s]], writes=[b_ubf[s]])
            p.op("scalar", lambda e, s=s, il=il: e.activation(
                out=dwbf[:, il * 2048:(il + 1) * 2048], in_=dw8[s][:, :],
                func=AF.Copy, bias=-128.0),
                reads=[b_dw8[s]], writes=[b_dwbf])
            for n2 in range(2):
                for kc in range(KC):
                    p.op("tensor", lambda e, s=s, n2=n2, kc=kc: e.matmul(
                        psA[:, n2 * 512:(n2 + 1) * 512],
                        gbf[s][:, kc * 128:(kc + 1) * 128],
                        xn_ffn[:, kc * 1024 + n2 * 512:kc * 1024 + (n2 + 1) * 512],
                        start=(kc == 0), stop=(kc == KC - 1)),
                        reads=[b_gbf[s], b_xnffn], writes=[b_psA_lo])
            for n2 in range(2):
                for kc in range(KC):
                    p.op("tensor", lambda e, s=s, n2=n2, kc=kc: e.matmul(
                        psA[:, 1024 + n2 * 512:1024 + (n2 + 1) * 512],
                        ubf[s][:, kc * 128:(kc + 1) * 128],
                        xn_ffn[:, kc * 1024 + n2 * 512:kc * 1024 + (n2 + 1) * 512],
                        start=(kc == 0), stop=(kc == KC - 1)),
                        reads=[b_ubf[s], b_xnffn], writes=[b_psA_hi])
            p.op("scalar", lambda e, s=s, it=it: e.activation(
                out=gsil[s][:, :], in_=psA[:, 0:1024], func=AF.Silu,
                scale=gs_sb[:, it:it + 1]),
                reads=[b_psA_lo], scalar_reads=[b_gs], writes=[b_gsil[s]])
            p.op("vector", lambda e, s=s, it=it, il=il: e.scalar_tensor_tensor(
                out=hact[:, il * 1024:(il + 1) * 1024], in0=psA[:, 1024:2048],
                scalar=us_sb[:, it:it + 1], in1=gsil[s][:, :],
                op0=ALU.mult, op1=ALU.mult),
                reads=[b_psA_hi, b_gsil[s]], scalar_reads=[b_us], writes=[b_hact])
        for tt in range(NH):
            for n4 in range(4):
                breg = b_psBb[n4]
                for il in range(GRP):
                    p.op("tensor", lambda e, tt=tt, n4=n4, il=il: e.matmul(
                        psB[:, n4 * 512:(n4 + 1) * 512],
                        hact[:, il * 1024 + tt * 128:il * 1024 + (tt + 1) * 128],
                        dwbf[:, il * 2048 + n4 * 512:il * 2048 + (n4 + 1) * 512],
                        start=(il == 0), stop=(il == GRP - 1)),
                        reads=[b_hact, b_dwbf], writes=[breg])
            for n4 in range(4):
                if g == 0:
                    p.op("vector", lambda e, tt=tt, n4=n4: e.tensor_copy(
                        out=out_acc[:, tt * 2048 + n4 * 512:
                                    tt * 2048 + (n4 + 1) * 512],
                        in_=psB[:, n4 * 512:(n4 + 1) * 512]),
                        reads=[b_psBb[n4]], writes=[b_acc])
                else:
                    p.op("vector", lambda e, tt=tt, n4=n4: e.tensor_add(
                        out_acc[:, tt * 2048 + n4 * 512:
                                tt * 2048 + (n4 + 1) * 512],
                        out_acc[:, tt * 2048 + n4 * 512:
                                tt * 2048 + (n4 + 1) * 512],
                        psB[:, n4 * 512:(n4 + 1) * 512]),
                        reads=[b_psBb[n4], b_acc], writes=[b_acc])

    mkE3 = roll("E")
    b_xr = mkE3("xr")
    b_ta = mkE3("ta")
    b_tb = mkE3("tb")
    b_sat = [mkE3("sat0"), mkE3("sat1")]
    b_dsb = mkE3("dsb")
    p.dma("sync", lambda e: e.dma_start(out=dsb_sb[:, :], in_=dsb_d[:, :]),
          writes=[b_dsb])
    out_evs = []
    for h in range(NH):
        s = h % 2
        p.dma("sync", lambda e, h=h: e.dma_start(
            out=xr[:, :], in_=x_d[h * 128:(h + 1) * 128, :]),
            writes=[b_xr])
        p.dma("sync", lambda e, h=h, s=s: e.dma_start(
            out=sat[s][:, :], in_=satscr[h * 128:(h + 1) * 128, :]),
            reads=[b_satscr], writes=[b_sat[s]])
        p.op("vector", lambda e, h=h: e.tensor_mul(
            ta[:, :], out_acc[:, h * 2048:(h + 1) * 2048], dsb_sb[:, :]),
            reads=[b_acc, b_dsb], writes=[b_ta])
        p.op("scalar", lambda e: e.activation(out=tb[:, :], in_=ta[:, :],
                                              func=AF.Silu),
             reads=[b_ta], writes=[b_tb])
        p.op("vector", lambda e, s=s: e.tensor_mul(ta[:, :], tb[:, :], sat[s][:, :]),
             reads=[b_tb, b_sat[s]], writes=[b_ta])
        p.op("vector", lambda e: e.tensor_add(tb[:, :], ta[:, :], xr[:, :]),
             reads=[b_ta, b_xr], writes=[b_tb])
        ev = p.dma("scalar", lambda e, h=h: e.dma_start(
            out=out_d[h * 128:(h + 1) * 128, :], in_=tb[:, :]),
            reads=[b_tb])
        out_evs.append(ev)
    p.op("gpsimd", lambda e: e.engine_nop(), deps=out_evs)

    semvals, nwaits = p.finalize()
    nops = len(p.ops)
    with ExitStack() as stack:
        sems = {n: stack.enter_context(nc.semaphore("s_" + n)) for n in ENGINES}
        with nc.Block() as block:
            run = p.emit(sems)
            for n in ENGINES:
                getattr(block, n)((lambda nn: lambda e: run(e, nn))(n))
    return nc, nops, nwaits, semvals


def _prep_inputs(inputs):
    x = np.ascontiguousarray(inputs["L0_attn_x"])
    wq = inputs["L0_attn_wq"]
    wk = inputs["L0_attn_wk"]
    wv = inputs["L0_attn_wv"]
    wo = inputs["L0_attn_wo"]

    def u8view(a):
        a = np.ascontiguousarray(a.astype(np.int32, copy=False))
        return a.view(np.uint8).reshape(a.shape[0], a.shape[1], 4)[:, :, 0].copy()

    gw = u8view(inputs["L0_ffn_gate_w"])
    uw = u8view(inputs["L0_ffn_up_w"])
    dw = u8view(inputs["L0_ffn_down_w"])

    wq_r = wq.reshape(16, 128, 16, 128).transpose(0, 3, 2, 1)
    wk_r = wk.reshape(16, 128, 16, 128).transpose(0, 3, 2, 1)
    wv_r = wv.reshape(16, 128, 16, 128).transpose(0, 3, 2, 1)
    gw_r = np.ascontiguousarray(gw.reshape(64, 128, 16, 128).transpose(0, 3, 2, 1))
    uw_r = np.ascontiguousarray(uw.reshape(64, 128, 16, 128).transpose(0, 3, 2, 1))
    dw_r = np.ascontiguousarray(dw.T.reshape(64, 128, 2048))
    gs_r = np.ascontiguousarray(inputs["L0_ffn_gate_scale"].reshape(64, 128).T)
    us_r = np.ascontiguousarray(inputs["L0_ffn_up_scale"].reshape(64, 128).T)
    dsb = np.ascontiguousarray(
        np.broadcast_to(inputs["L0_ffn_down_scale"][None, :], (128, 2048)))
    lam4 = np.ascontiguousarray(np.stack(
        [inputs["L0_attn_lambda_q1"], inputs["L0_attn_lambda_q2"],
         inputs["L0_attn_lambda_k1"], inputs["L0_attn_lambda_k2"]], axis=1))

    perms, wo_par, wq_par, wk_par, wv2_par = [], [], [], [], []
    hd = np.arange(128)
    for parity in range(2):
        h0 = 8 * parity
        perm = np.empty(2048, np.int64)
        for half, hh0 in enumerate([h0, (h0 + 8) % 16]):
            for hl in range(8):
                perm[half * 1024 + hl * 128 + hd] = 16 * hd + hh0 + hl
        perms.append(perm)
        wop = np.ascontiguousarray(wo.T[perm])
        wo_par.append(np.ascontiguousarray(
            wop.reshape(16, 128, 2, 1024).transpose(2, 0, 1, 3)))
        wq_par.append(np.ascontiguousarray(wq_r[h0:h0 + 8]))
        wk_par.append(np.ascontiguousarray(wk_r[h0:h0 + 8]))
        wv2_par.append(np.ascontiguousarray(
            wv_r[h0:h0 + 8].reshape(4, 2, 128, 16, 128)
            .transpose(0, 2, 3, 1, 4).reshape(4, 128, 16, 256)))

    in_maps = []
    for core in range(8):
        b, parity = core // 2, core % 2
        x_perm = np.ascontiguousarray(x[b][perms[parity]])
        in_maps.append({
            "x": x_perm,
            "wq": wq_par[parity], "wk": wk_par[parity], "wv2": wv2_par[parity],
            "wo": wo_par[parity],
            "gw": gw_r, "uw": uw_r, "dw": dw_r,
            "gs": gs_r, "us": us_r, "dsb": dsb, "lam4": lam4,
        })
    return in_maps, perms


def kernel(**inputs):
    in_maps, perms = _prep_inputs(inputs)
    nc, nops, nwaits, semvals = build_nc()
    res = run_bass_kernel_spmd(nc, in_maps, core_ids=list(range(8)))
    out = np.empty((B, S, DM), dtype=np.float32)
    for core in range(8):
        b, parity = core // 2, core % 2
        r = res.results[core]["out"]
        out[b][perms[parity][:1024]] = r
    return out
